# version 23
# baseline (speedup 1.0000x reference)
import numpy as np
import ml_dtypes
import concourse.bass as bass
import concourse.mybir as mybir
from concourse.bass_utils import run_bass_kernel_spmd

F32 = mybir.dt.float32
BF16 = mybir.dt.bfloat16
AF = mybir.ActivationFunctionType
ALU = mybir.AluOpType
AXX = mybir.AxisListType.X

RC = float(3 * 2 ** 22)
EPS = 1e-5
QB = 127.0
NEG = -1e9

ENGINES = ("sync", "scalar", "vector", "tensor", "gpsimd")


class Builder:

    def __init__(self, nc, stack):
        self.nc = nc
        self.ops = {e: [] for e in ENGINES}
        self.count = {e: 0 for e in ENGINES}
        self.esem = {e: stack.enter_context(nc.semaphore(f"es_{e}")) for e in ENGINES}
        self.pend_r = {e: [] for e in ENGINES}
        self.pend_w = {e: [] for e in ENGINES}
        self.writers = {}
        self.readers = {}
        self.watermark = {e: {} for e in ENGINES}
        self.pools = {}
        for q, n in (("sync", 7), ("gpsimd", 7), ("scalar", 5)):
            self.pools[q] = {"sems": [stack.enter_context(nc.semaphore(f"dp_{q}{i}"))
                                      for i in range(n)], "n": 0}

    def _wait_of(self, h):
        if h[0] == "eng":
            return (self.esem[h[1]], h[2])
        return (h[1], h[2])

    def _collect(self, eng, reads, writes, accum_writes):
        need = []
        for r in reads:
            need += self.writers.get(r, [])
        for w in writes:
            need += self.writers.get(w, [])
            need += self.readers.get(w, [])
        for w in accum_writes:
            need += self.readers.get(w, [])
        waits = {}
        for h in need:
            sem, val = self._wait_of(h)
            k = id(sem)
            if self.watermark[eng].get(k, 0) >= val:
                continue
            if waits.get(k, (None, 0))[1] < val:
                waits[k] = (sem, val)
        out = list(waits.values())
        for sem, val in out:
            self.watermark[eng][id(sem)] = max(
                self.watermark[eng].get(id(sem), 0), val)
        return out

    def _update(self, handle, reads, writes, accum_writes):
        for r in reads:
            self.readers.setdefault(r, []).append(handle)
        for w in writes:
            self.writers[w] = [handle]
            self.readers[w] = []
        for w in accum_writes:
            self.writers.setdefault(w, []).append(handle)

    def op(self, eng, fn, reads=(), writes=(), accum_writes=(), inc=True):
        waits = self._collect(eng, reads, writes, accum_writes)
        if inc:
            self.count[eng] += 1
            h = ("eng", eng, self.count[eng])
            pr = self.pend_r[eng] + list(reads)
            pw = self.pend_w[eng] + list(writes)
            self.pend_r[eng] = []
            self.pend_w[eng] = []
            self._update(h, pr, pw, accum_writes)
            self.ops[eng].append((waits, fn, (self.esem[eng], 1)))
        else:
            assert not accum_writes
            self.pend_r[eng] += list(reads)
            self.pend_w[eng] += list(writes)
            self.ops[eng].append((waits, fn, None))

    def dma(self, q, out_ap, in_ap, reads=(), writes=(), accum_writes=(),
            transpose=False):
        pool = self.pools[q]
        i = pool["n"]
        pool["n"] += 1
        sems = pool["sems"]
        sem = sems[i % len(sems)]
        cyc = i // len(sems)
        waits = []
        if cyc > 0 and self.watermark[q].get(id(sem), 0) < 16 * cyc:
            waits.append((sem, 16 * cyc))
            self.watermark[q][id(sem)] = 16 * cyc
        waits += self._collect(q, reads, writes, accum_writes)
        h = ("sem", sem, 16 * (cyc + 1))
        self._update(h, reads, writes, accum_writes)
        if transpose:
            fn = lambda e, o=out_ap, ii=in_ap: e.dma_start_transpose(o, ii)
        else:
            fn = lambda e, o=out_ap, ii=in_ap: e.dma_start(o, ii)
        self.ops[q].append((waits, fn, (sem, 16)))

    def _pool_targets(self):
        t = []
        for q, pool in self.pools.items():
            for j, s in enumerate(pool["sems"]):
                uses = (pool["n"] - 1 - j) // len(pool["sems"]) + 1 \
                    if pool["n"] > j else 0
                if uses > 0:
                    t.append((s, 16 * uses))
        return t

    def barrier(self):
        targets = [(self.esem[e2], self.count[e2]) for e2 in ENGINES
                   if self.count[e2] > 0]
        targets += self._pool_targets()
        for e in ENGINES:
            waits = []
            for sem, val in targets:
                if self.watermark[e].get(id(sem), 0) < val:
                    waits.append((sem, val))
                    self.watermark[e][id(sem)] = val
            if waits:
                self.ops[e].append((waits, None, None))

    def drain_dma(self, eng="sync"):
        for sem, val in self._pool_targets():
            if self.watermark[eng].get(id(sem), 0) < val:
                self.ops[eng].append(([(sem, val)], None, None))
                self.watermark[eng][id(sem)] = val

    def emit(self, block):
        def replay(name):
            def go(eng):
                for waits, fn, incspec in self.ops[name]:
                    for sem, val in waits:
                        eng.wait_ge(sem, val)
                    if fn is None:
                        continue
                    inst = fn(eng)
                    if incspec is not None:
                        inst.then_inc(incspec[0], incspec[1])
            return go
        block.sync(replay("sync"))
        block.scalar(replay("scalar"))
        block.vector(replay("vector"))
        block.tensor(replay("tensor"))
        block.gpsimd(replay("gpsimd"))


def build_nc(NT=32, npart=None):
    T = NT * 128
    D = 1024
    H = 4096
    NMAC = max(1, NT // 4)
    NPART = npart if npart else (2 if NT >= 8 else 1)
    import contextlib
    stack = contextlib.ExitStack()
    nc = bass.Bass("TRN2", target_bir_lowering=False)

    enc = nc.dram_tensor("enc", [T, D], F32, kind="ExternalInput")
    r_mat = nc.dram_tensor("r_mat", [D, D], F32, kind="ExternalInput")
    w1t = nc.dram_tensor("w1t", [D, H], F32, kind="ExternalInput")
    w2t = nc.dram_tensor("w2t", [H, D], F32, kind="ExternalInput")
    b1bc = nc.dram_tensor("b1bc", [128, H], F32, kind="ExternalInput")
    b2bc = nc.dram_tensor("b2bc", [128, D], F32, kind="ExternalInput")
    maskc = nc.dram_tensor("maskc", [128, 896], F32, kind="ExternalInput")
    identi = nc.dram_tensor("identi", [128, 128], BF16, kind="ExternalInput")
    ones_i = nc.dram_tensor("ones_i", [128, 1], F32, kind="ExternalInput")
    out = nc.dram_tensor("out", [T, D], F32, kind="ExternalOutput")

    enc_bf = nc.dram_tensor("enc_bf", [T, D], BF16, kind="Internal")
    r_bfT = nc.dram_tensor("r_bfT", [8, 128, D], BF16, kind="Internal")
    y1pre_h = nc.dram_tensor("y1pre_h", [T, D], F32, kind="Internal")
    y2pre_h = nc.dram_tensor("y2pre_h", [T, D], BF16, kind="Internal")
    w1q_h = nc.dram_tensor("w1q_h", [D, H], BF16, kind="Internal")
    w2q_h = nc.dram_tensor("w2q_h", [H, D], BF16, kind="Internal")

    BIGW = max(NT * 2048, 65536)
    with stack:
        bf_big = stack.enter_context(nc.sbuf_tensor("bf_big", [128, BIGW], BF16))
        bf_med = stack.enter_context(nc.sbuf_tensor("bf_med", [128, 16384], BF16))
        f32_big = stack.enter_context(nc.sbuf_tensor("f32_big", [128, 8192], F32))
        b1b = stack.enter_context(nc.sbuf_tensor("b1b", [128, H], BF16))
        rows2 = stack.enter_context(nc.sbuf_tensor("rows2", [2, 1536], BF16))
        hl = stack.enter_context(nc.sbuf_tensor("hl", [128, 4], BF16))
        mask_sb = stack.enter_context(nc.sbuf_tensor("mask_sb", [128, 896], BF16))
        ident = stack.enter_context(nc.sbuf_tensor("ident", [128, 128], BF16))
        ones_sb = stack.enter_context(nc.sbuf_tensor("ones_sb", [128, 1], F32))
        stats = stack.enter_context(nc.sbuf_tensor("stats", [128, 448], F32))
        psc = [stack.enter_context(nc.psum_tensor(f"psc{i}", [128, 512], F32))
               for i in range(4)]
        pav = stack.enter_context(nc.psum_tensor("pav", [128, 1024], F32))
        ptp = [stack.enter_context(nc.psum_tensor(f"ptp{i}", [128, 1024], BF16))
               for i in range(2)]
        block = stack.enter_context(nc.Block())
        b = Builder(nc, stack)

        def etm(k):
            return bf_big[:, 1024 * k:1024 * (k + 1)]

        def efm(c, lo, hi):
            base = NT * 1024 + NT * 128 * c
            return bf_big[:, base + lo:base + hi]
        def w1tb(d):
            return bf_big[:, 4096 * d:4096 * (d + 1)]

        def w2tb(h):
            return bf_big[:, 32768 + 1024 * h:32768 + 1024 * (h + 1)]
        def encR(e):
            return bf_med[:, 512 * e:512 * (e + 1)]

        def A_sb(s):
            return bf_med[:, 4096 + NT * 128 * s:4096 + NT * 128 * (s + 1)]
        abase = 4096 + 2 * NT * 128

        def ATg(s):
            if s < 2:
                return bf_med[:, abase + 512 * s:abase + 512 * (s + 1)]
            return b1b[:, 512 * (s - 2):512 * (s - 1)]
        rstage = bf_med[:, abase + 1024:abase + 2048]
        rstage2 = b1b[:, 3072:4096]

        def Rcol(s):
            return bf_med[:, abase + 2048 + 1024 * s:abase + 2048 + 1024 * (s + 1)]
        hq_fm = bf_med[:, 0:4096]
        xq_tm = bf_med[:, 4096:5120]
        xq_fm = bf_med[:, 5120:6144]

        def y2st(s):
            return bf_med[:, 6144 + 1024 * s:6144 + 1024 * (s + 1)]

        def hq_tm(s):
            return bf_med[:, 8192 + 4096 * s:8192 + 4096 * (s + 1)]
        def S_sl(lo, hi):
            return f32_big[:, lo:hi]

        def eres(s):
            return f32_big[:, 4096 + 1024 * s:4096 + 1024 * (s + 1)]

        def y1p(s):
            return f32_big[:, 6144 + 1024 * s:6144 + 1024 * (s + 1)]
        def wfp(s):
            return f32_big[:, 4096 * s:4096 * (s + 1)]
        def hbuf(lo, hi):
            return f32_big[:, lo:hi]

        def y1t(s):
            off = [4096, 5120, 7168][s]
            return f32_big[:, off:off + 1024]
        scr1 = f32_big[:, 6144:7168]

        def outb(s):
            return f32_big[:, 1024 * s:1024 * (s + 1)]

        SC = {}
        _sc = [0]

        def st(name, w=1):
            assert name not in SC
            SC[name] = _sc[0]
            _sc[0] += w
            o = SC[name]
            return stats[:, o:o + w]
        smax = st("smax", 8)
        sums4 = st("sums4", 8)
        bcn = st("bcn")
        sums = st("sums", 2)
        rcp = st("rcp", 2)
        rows = st("rows")
        ssq = st("ssq")
        t2 = st("t2")
        t3 = st("t3")
        mA = st("mA", NT)
        uA = st("uA", NT)
        mB = st("mB", NT)
        uB = st("uB", NT)
        rstdA = st("rstdA", NT)
        nmrA = st("nmrA", NT)
        rstdB = st("rstdB", NT)
        nmrB = st("nmrB", NT)
        lnu = st("lnu", NT)
        colab = st("colab", 32)
        colab2 = st("colab2", 32)
        w1sb = st("w1sb")
        w1si = st("w1si")
        w1hi = st("w1hi")
        w1lo = st("w1lo")
        w2sb = st("w2sb")
        w2si = st("w2si")
        w2hi = st("w2hi")
        w2lo = st("w2lo")
        habs = st("habs", 8)
        xsr = st("xsr")
        cx = st("cx")
        s1c = st("s1c", 2)
        hsr = st("hsr")
        ch = st("ch")
        s2c = st("s2c", 2)
        csum = st("csum")
        rcd = st("rcd")
        assert _sc[0] <= 448

        def part_of(tok):
            return 0 if NPART == 1 else (0 if tok < T // 2 else 1)

        b.op("vector", lambda v: v.memset(bcn, -12.0), writes=["bcn"])
        b.op("vector", lambda v: v.memset(rcd, 1.0), writes=["rcd"])
        b.op("vector", lambda v: v.tensor_scalar(
            rcd, rcd, 2.0 ** -24, 1.0, ALU.add, ALU.subtract),
            reads=["rcd"], writes=["rcd"])
        b.op("vector", lambda v: v.tensor_scalar(
            rcd, rcd, 2.0 ** 24, None, ALU.mult),
            reads=["rcd"], writes=["rcd"])
        b.op("vector", lambda v: v.tensor_scalar(
            rcd, rcd, 6755399441055744.0, RC, ALU.mult, ALU.add),
            reads=["rcd"], writes=["rcd"])


        NG = NT // 4

        def emit_enc_loads(g):
            if not (0 <= g < NG):
                return
            for k in range(4 * g, 4 * g + 4):
                b.dma("gpsimd", etm(k), enc[128 * k:128 * (k + 1), :],
                      writes=[f"etm{k}"])

        def emit_enc_stores(g):
            if not (0 <= g < NG):
                return
            for k in range(4 * g, 4 * g + 4):
                b.dma("sync", enc_bf[128 * k:128 * (k + 1), :], etm(k),
                      reads=[f"etm{k}"], accum_writes=[f"encbf{g}"])

        def emit_enc_transp(g):
            if not (0 <= g < NG):
                return
            for c in range(8):
                b.dma("sync", efm(c, 512 * g, 512 * (g + 1)),
                      enc_bf[512 * g:512 * (g + 1), 128 * c:128 * (c + 1)],
                      reads=[f"encbf{g}"], writes=[f"efm{c}g{g}"],
                      transpose=True)

        b.dma("gpsimd", mask_sb[:, :], maskc[:, :], writes=["mask"])
        emit_enc_loads(0)
        b.dma("sync", ident[:, :], identi[:, :], writes=["ident"])
        b.dma("sync", ones_sb[:, :], ones_i[:, :], writes=["ones"])
        for d in range(8):
            stg, rg = (rstage, "rstage") if d % 2 == 0 else (rstage2, "rstage2")
            b.dma("gpsimd", stg[:, :], r_mat[128 * d:128 * (d + 1), :],
                  writes=[rg])
            b.dma("scalar",
                  r_bfT[:, :, 128 * d:128 * (d + 1)].transpose([1, 0, 2]),
                  stg.rearrange("p (e j) -> p e j", e=8),
                  reads=[rg], accum_writes=["rbfT"])
        emit_enc_loads(1)
        emit_enc_stores(0)
        emit_enc_transp(0)
        b.dma("sync", Rcol(0), r_bfT[0], reads=["rbfT"], writes=["Rcol0"])
        emit_enc_loads(2)
        emit_enc_stores(1)
        emit_enc_transp(1)
        emit_enc_loads(3)
        emit_enc_stores(2)
        b.dma("gpsimd", eres(0), enc[0:128, :], writes=["eres0"])
        rload = [1]

        def wslot(s):
            return f32_big[:, 1024 * s:1024 * (s + 1)]

        def w1_chunk_aps(ci):
            r, q = divmod(ci, 4)
            src_ = w1t[128 * r:128 * (r + 1), 1024 * q:1024 * (q + 1)]
            dst = w1q_h[128 * r:128 * (r + 1), 1024 * q:1024 * (q + 1)]
            return src_, dst

        def w2_chunk_aps(ci):
            return (w2t[128 * ci:128 * (ci + 1), :],
                    w2q_h[128 * ci:128 * (ci + 1), :])

        b2row2 = rows2[:, 0:1024]
        def s1row2(s):
            return rows2[:, 1024 + 128 * s:1024 + 128 * (s + 1)]
        def s2row2(s):
            return rows2[:, 1280 + 128 * s:1280 + 128 * (s + 1)]
        b1row2 = b1b[0:2, :]
        s1hl = hl[:, 0:2]
        s2hl = hl[:, 2:4]

        prep_loads = []
        prep_comps = []
        prep_stores = []

        def add_p1(caps, ci, col_ap, colrg, k):
            sl = k % 3
            prep_loads.append(lambda: b.dma(
                "gpsimd", wslot(sl), caps(ci)[0], writes=[f"wsl{sl}"]))
            prep_comps.append(lambda: b.op("vector", lambda v: v.reduce_sum(
                col_ap, wslot(sl), axis=AXX, apply_absolute_value=True),
                reads=[f"wsl{sl}"], accum_writes=[colrg]))

        def add_scales(qn, col_all, colrg, nelem, sb_sc, sb_inv, sb_hi, sb_lo):
            def step():
                b.op("vector", lambda v: v.reduce_sum(
                    csum, col_all, axis=AXX), reads=[colrg], writes=["csum"])
                b.op("tensor", lambda t: t.matmul(
                    psc[3][0:1, 0:1], ones_sb[:, :], csum,
                    start=True, stop=True),
                    reads=["ones", "csum"], writes=["psc3"])
                b.op("vector", lambda v: v.tensor_scalar(
                    csum[0:1, 0:1], psc[3][0:1, 0:1], 1.0 / nelem, EPS,
                    ALU.mult, ALU.max), reads=["psc3"], writes=["csum"])
                b.op("tensor", lambda t: t.matmul(
                    psc[3][:, 0:1], ones_sb[0:1, 0:1].broadcast_to((1, 128)),
                    csum[0:1, 0:1], start=True, stop=True),
                    reads=["ones", "csum"], writes=["psc3"])
                b.op("vector", lambda v: v.tensor_copy(sb_sc, psc[3][:, 0:1]),
                     reads=["psc3"], writes=[f"{qn}sb"])
                b.op("vector", lambda v: v.reciprocal(sb_inv, sb_sc),
                     reads=[f"{qn}sb"], writes=[f"{qn}si"])
                b.op("vector", lambda v: v.tensor_scalar(
                    sb_hi, sb_sc, 1.0, None, ALU.mult),
                    reads=[f"{qn}sb"], writes=[f"{qn}hi"])
                b.op("vector", lambda v: v.tensor_scalar(
                    sb_lo, sb_sc, -1.0, None, ALU.mult),
                    reads=[f"{qn}sb"], writes=[f"{qn}lo"])
            prep_loads.append(None)
            prep_comps.append(step)

        def add_p2(qn, caps, ci, sb_inv, sb_hi, sb_lo, k):
            sl = k % 3
            prep_loads.append(lambda: b.dma(
                "gpsimd", wslot(sl), caps(ci)[0], writes=[f"wsl{sl}"]))

            def step():
                b.op("vector", lambda v: v.tensor_scalar(
                    wslot(sl), wslot(sl), sb_hi, sb_lo, ALU.min, ALU.max),
                    reads=[f"wsl{sl}", f"{qn}hi", f"{qn}lo"],
                    writes=[f"wsl{sl}"])
                b.op("vector", lambda v: v.tensor_scalar(
                    wslot(sl), wslot(sl), sb_inv, rcd, ALU.mult, ALU.add),
                    reads=[f"wsl{sl}", f"{qn}si", "rcd"], writes=[f"wsl{sl}"])
                b.op("vector", lambda v: v.tensor_scalar(
                    wslot(sl), wslot(sl), rcd, None, ALU.subtract),
                    reads=[f"wsl{sl}", "rcd"], writes=[f"wsl{sl}"])
                prep_stores.append(lambda: b.dma(
                    "gpsimd", caps(ci)[1], wslot(sl), reads=[f"wsl{sl}"],
                    accum_writes=[f"{qn}qh"]))
            prep_comps.append(step)

        k = 0
        for ci in range(32):
            add_p1(w1_chunk_aps, ci, colab[:, ci:ci + 1], "colab", k)
            k += 1
        add_scales("w1", colab[:, 0:32], "colab", D * H, w1sb, w1si, w1hi, w1lo)
        k += 1
        for ci in range(32):
            add_p1(w2_chunk_aps, ci, colab2[:, ci:ci + 1], "colab2", k)
            k += 1
        add_scales("w2", colab2[:, 0:32], "colab2", D * H, w2sb, w2si, w2hi,
                   w2lo)
        k += 1
        for ci in range(32):
            add_p2("w1", w1_chunk_aps, ci, w1si, w1hi, w1lo, k)
            k += 1
        for ci in range(32):
            add_p2("w2", w2_chunk_aps, ci, w2si, w2hi, w2lo, k)
            k += 1
        NP = len(prep_comps)
        prep_c = [0]
        prep_l = [0]
        store_q = []

        def _flush_stores(upto):
            while store_q and store_q[0][0] <= upto:
                store_q.pop(0)[1]()

        def _issue_load(j):
            _flush_stores(j - 3)
            if prep_loads[j] is not None:
                prep_loads[j]()

        def drain_prep(frac):
            ct = min(int(round(NP * frac)), NP)
            while prep_c[0] < ct:
                kk = prep_c[0]
                while prep_l[0] <= kk:
                    _issue_load(prep_l[0])
                    prep_l[0] += 1
                nst = len(prep_stores)
                prep_comps[kk]()
                for fn in prep_stores[nst:]:
                    store_q.append((kk, fn))
                del prep_stores[nst:]
                prep_c[0] += 1
            while prep_l[0] < min(ct + 2, NP):
                _issue_load(prep_l[0])
                prep_l[0] += 1
            if ct >= NP:
                _flush_stores(NP)

        def emit_encR(m):
            qlo = 512 * m
            for e in range(8):
                if e + 1 < 8 or m + 1 < NMAC:
                    ne = (e + 1) % 8
                    sl = rload[0] % 2
                    b.dma("sync", Rcol(sl), r_bfT[ne], reads=["rbfT"],
                          writes=[f"Rcol{sl}"])
                    rload[0] += 1
                cur = (m * 8 + e) % 2
                bk = 2 + e % 2
                for d in range(8):
                    b.op("tensor",
                         lambda t, e=e, d=d, cur=cur, qlo=qlo, bk=bk: t.matmul(
                             psc[bk][:, :], Rcol(cur)[:, 128 * d:128 * (d + 1)],
                             efm(d, qlo, qlo + 512),
                             start=(d == 0), stop=(d == 7)),
                         reads=[f"Rcol{cur}", f"efm{d}g{m}"],
                         writes=[f"psc{bk}"], inc=(d == 7))
                b.op("scalar", lambda s, e=e, bk=bk: s.copy(encR(e), psc[bk][:, :]),
                     reads=[f"psc{bk}"], writes=[f"encR{e}"])

        def emit_scores_super(qi, sup):
            nsup = (qi + 4) // 4
            off = 128 * qi - 512 * (nsup - 1)
            bank = sup % 4
            p = qi % 2
            for e in range(8):
                b.op("tensor",
                     lambda t, e=e, sup=sup, bank=bank, qi=qi: t.matmul(
                         psc[bank][:, :],
                         encR(e)[:, 128 * (qi % 4):128 * (qi % 4 + 1)],
                         efm(e, 512 * sup, 512 * (sup + 1)),
                         start=(e == 0), stop=(e == 7)),
                     reads=[f"encR{e}", f"efm{e}g{sup}"],
                     writes=[f"psc{bank}"], inc=(e == 7))
            if sup == nsup - 1:
                b.op("vector", lambda v, bank=bank, off=off: v.tensor_tensor(
                    psc[bank][:, :], psc[bank][:, :],
                    mask_sb[:, 384 - off:896 - off], ALU.add),
                    reads=[f"psc{bank}", "mask"], writes=[f"psc{bank}"])
            b.op("scalar", lambda s, sup=sup, bank=bank, p=p:
                 s.activation(A_sb(p)[:, 512 * sup:512 * (sup + 1)],
                              psc[bank][:, :], AF.Exp, bias=bcn,
                              accum_out=sums4[:, sup:sup + 1]),
                 reads=[f"psc{bank}", "bcn"], accum_writes=[f"A{p}", "sums4"])

        def emit_softmax_tail(qi):
            nsup = (qi + 4) // 4
            sl = qi % 2
            b.op("vector", lambda v, nsup=nsup, sl=sl: v.reduce_sum(
                sums[:, sl:sl + 1], sums4[:, 0:nsup], axis=AXX),
                reads=["sums4"], writes=[f"sums{sl}"])
            b.op("vector", lambda v, sl=sl: v.reciprocal(
                rcp[:, sl:sl + 1], sums[:, sl:sl + 1]),
                reads=[f"sums{sl}"], writes=[f"rcp{sl}"])

        def tgroup(src_ap, src_region, blocks, pk, dst_ap, dst_region, accum=False):
            for j, bb in enumerate(blocks):
                b.op("tensor", lambda t, j=j, bb=bb: t.transpose(
                    ptp[pk][:, 128 * j:128 * (j + 1)],
                    src_ap[:, 128 * bb:128 * (bb + 1)], ident[:, :]),
                    reads=[src_region, "ident"], accum_writes=[f"ptp{pk}"])
            n = 128 * len(blocks)
            dst2 = dst_ap[:, 0:n] if dst_ap.shape[1] > n else dst_ap
            if accum:
                b.op("scalar", lambda s, n=n: s.copy(dst2, ptp[pk][:, 0:n]),
                     reads=[f"ptp{pk}"], accum_writes=[dst_region])
            else:
                b.op("scalar", lambda s, n=n: s.copy(dst2, ptp[pk][:, 0:n]),
                     reads=[f"ptp{pk}"], writes=[dst_region])

        def emit_avM(qi, groups, glo, ghi):
            nb = qi + 1
            for g in range(glo, min(ghi, len(groups))):
                for j, bb in enumerate(groups[g]):
                    b.op("tensor", lambda t, j=j, bb=bb, g=g, nb=nb: t.matmul(
                        pav[:, 0:512], ATg(g)[:, 128 * j:128 * (j + 1)],
                        etm(bb)[:, 0:512],
                        start=(bb == 0), stop=(bb == nb - 1)),
                        reads=[f"ATG{g}", f"etm{bb}"], writes=["pav0"],
                        inc=(bb == nb - 1))
                    b.op("tensor", lambda t, j=j, bb=bb, g=g, nb=nb: t.matmul(
                        pav[:, 512:1024], ATg(g)[:, 128 * j:128 * (j + 1)],
                        etm(bb)[:, 512:1024],
                        start=(bb == 0), stop=(bb == nb - 1)),
                        reads=[f"ATG{g}", f"etm{bb}"], writes=["pav1"],
                        inc=(bb == nb - 1))

        def emit_epiA(qi):
            sl = qi % 2
            b.op("vector", lambda v, sl=sl: v.scalar_tensor_tensor(
                y1p(sl), pav[:, :], rcp[:, sl:sl + 1], eres(sl),
                ALU.mult, ALU.add, accum_out=rows),
                reads=["pav0", "pav1", f"rcp{sl}", f"eres{sl}"],
                writes=[f"y1p{sl}", "rows"])
            b.op("scalar", lambda s, sl=sl: s.activation(
                rstage, y1p(sl), AF.Square, accum_out=ssq),
                reads=[f"y1p{sl}"], writes=["rstage", "ssq"])
            b.op("vector", lambda v, qi=qi: v.tensor_scalar(
                mA[:, qi:qi + 1], rows, 1.0 / 1024.0, None, ALU.mult),
                reads=["rows"], accum_writes=["mA"])
            b.op("vector", lambda v, qi=qi: v.tensor_tensor(
                t2, mA[:, qi:qi + 1], mA[:, qi:qi + 1], ALU.mult),
                reads=["mA"], writes=["t2"])
            b.op("vector", lambda v: v.tensor_scalar(
                t3, ssq, 1.0 / 1024.0, None, ALU.mult),
                reads=["ssq"], writes=["t3"])
            b.op("vector", lambda v, qi=qi: v.scalar_tensor_tensor(
                uA[:, qi:qi + 1], t3, EPS, t2, ALU.add, ALU.subtract),
                reads=["t3", "t2"], accum_writes=["uA"])
            b.dma("sync", y1pre_h[128 * qi:128 * (qi + 1), :], y1p(sl),
                  reads=[f"y1p{sl}"], writes=[f"y1h{qi}"])

        for qi in range(NT + 1):
            nsup = (qi + 4) // 4 if qi < NT else 0
            if qi < NT and qi % 4 == 0:
                m = qi // 4
                emit_encR(m)
                emit_enc_loads(m + 4)
                emit_enc_stores(m + 3)
                emit_enc_transp(m + 2)
            if 1 <= qi < NT:
                b.dma("gpsimd", eres(qi % 2), enc[128 * qi:128 * (qi + 1), :],
                      writes=[f"eres{qi % 2}"])
            ngp, groups = 0, []
            if qi >= 1:
                nb = qi
                ngp = (nb + 3) // 4
                groups = [list(range(4 * g, min(4 * g + 4, nb)))
                          for g in range(ngp)]
                ar = f"A{(qi - 1) % 2}"
                Ap = A_sb((qi - 1) % 2)
            for s in range(max(nsup, ngp + 1)):
                if s < nsup:
                    emit_scores_super(qi, s)
                if s < ngp:
                    tgroup(Ap, ar, groups[s], s % 2, ATg(s), f"ATG{s}")
                if 0 <= s - 1 < ngp:
                    emit_avM(qi - 1, groups, s - 1, s)
            if qi >= 1:
                emit_epiA(qi - 1)
            if qi < NT:
                emit_softmax_tail(qi)
            if qi >= 1:
                drain_prep((2 * qi + 1) / (2.0 * NT))
            drain_prep((qi + 1) / (1.0 * NT))
        drain_prep(1.0)

        b.barrier()
        b.dma("gpsimd", b1row2, b1bc[0:2, :], writes=["b1row2"])
        b.dma("gpsimd", b2row2, b2bc[0:2, :], writes=["b2row2"])
        for d in range(8):
            b.dma("sync", w1tb(d), w1q_h[128 * d:128 * (d + 1), :],
                  reads=["w1qh"], writes=[f"w1q{d}"])
        for k in range(8):
            dst = bf_big[:, 32768 + 4096 * k:32768 + 4096 * (k + 1)].rearrange(
                "p (r c) -> p r c", r=4)
            src = w2q_h[512 * k:512 * (k + 1), :].rearrange(
                "(r p) c -> p r c", r=4, p=128)
            b.dma("sync", dst, src, reads=["w2qh"],
                  writes=[f"w2q{4 * k + j}" for j in range(4)])
        b.op("scalar", lambda s: s.activation(lnu[:, 0:NT], uA[:, 0:NT], AF.Ln),
             reads=["uA"], writes=["lnu"])
        b.op("scalar", lambda s: s.activation(rstdA[:, 0:NT], lnu[:, 0:NT],
                                              AF.Exp, scale=-0.5),
             reads=["lnu"], writes=["rstdA"])
        b.op("vector", lambda v: v.tensor_tensor(nmrA[:, 0:NT], mA[:, 0:NT],
                                                 rstdA[:, 0:NT], ALU.mult),
             reads=["mA", "rstdA"], writes=["nmrA"])
        b.op("vector", lambda v: v.tensor_scalar(nmrA[:, 0:NT], nmrA[:, 0:NT],
                                                 -1.0, None, ALU.mult),
             reads=["nmrA"], writes=["nmrA"])

        def emit_quant(i):
            sl = i % 3
            r = rstdA[:, i:i + 1]
            n = nmrA[:, i:i + 1]
            b.op("vector", lambda v, sl=sl: v.reduce_max(
                xsr, y1t(sl), axis=AXX), reads=[f"y1t{sl}", "rstdA"],
                writes=["xsr"])
            b.op("vector", lambda v, sl=sl: v.tensor_reduce(
                t2, y1t(sl), AXX, ALU.min), reads=[f"y1t{sl}"], writes=["t2"])
            b.op("vector", lambda v, r=r, n=n: v.tensor_scalar(
                xsr, xsr, r, n, ALU.mult, ALU.add),
                reads=["xsr", "nmrA"], writes=["xsr"])
            b.op("vector", lambda v, r=r, n=n: v.tensor_scalar(
                t2, t2, r, n, ALU.mult, ALU.add), reads=["t2"], writes=["t2"])
            b.op("vector", lambda v: v.tensor_scalar(
                t2, t2, -1.0, None, ALU.mult), reads=["t2"], writes=["t2"])
            b.op("vector", lambda v: v.tensor_tensor(
                xsr, xsr, t2, ALU.max), reads=["xsr", "t2"], writes=["xsr"])
            b.op("vector", lambda v: v.tensor_scalar(
                xsr, xsr, EPS, None, ALU.max), reads=["xsr"], writes=["xsr"])
            b.op("vector", lambda v: v.reciprocal(cx, xsr),
                 reads=["xsr"], writes=["cx"])
            b.op("vector", lambda v: v.tensor_scalar(
                cx, cx, QB, None, ALU.mult), reads=["cx"], writes=["cx"])
            b.op("vector", lambda v: v.tensor_scalar(
                t2, cx, w1si, None, ALU.mult),
                reads=["cx", "w1si"], writes=["t2"])
            b.op("vector", lambda v: v.tensor_copy(s1hl[:, 0:1], t2),
                 reads=["t2"], writes=["s1hl"])
            b.op("vector", lambda v: v.scalar_tensor_tensor(
                s1hl[:, 1:2], t2, 1.0, s1hl[:, 0:1], ALU.mult, ALU.subtract),
                reads=["t2", "s1hl"], writes=["s1hl"])
            b.op("vector", lambda v, i=i: v.tensor_scalar(
                s1c[:, i % 2:i % 2 + 1], xsr, w1sb, 1.0 / QB, ALU.mult,
                ALU.mult),
                reads=["xsr", "w1sb"], writes=[f"s1c{i % 2}"])
            b.op("vector", lambda v, r=r: v.tensor_tensor(
                t2, cx, r, ALU.mult), reads=["cx"], writes=["t2"])
            b.op("vector", lambda v, n=n: v.tensor_tensor(
                t3, cx, n, ALU.mult), reads=["cx", "nmrA"], writes=["t3"])
            b.op("vector", lambda v, sl=sl: v.tensor_scalar(
                scr1, y1t(sl), t2, t3, ALU.mult, ALU.add),
                reads=[f"y1t{sl}", "t2", "t3"], writes=["scr1"])
            b.op("vector", lambda v: v.tensor_scalar(
                xq_tm, scr1, rcd, rcd, ALU.add, ALU.subtract),
                reads=["scr1", "rcd"], writes=["xq_tm"])

        def emit_s1rows(i):
            b.op("tensor", lambda t: t.transpose(
                ptp[0][0:2, 0:128], s1hl, ident[:, :]),
                reads=["s1hl", "ident"], writes=["ptp0"])
            b.op("scalar", lambda s, i=i: s.copy(s1row2(i % 2),
                                                 ptp[0][0:2, 0:128]),
                 reads=["ptp0"], writes=[f"s1row2{i % 2}"])

        def emit_xqT(i):
            for g in range(2):
                tgroup(xq_tm, "xq_tm", list(range(4 * g, 4 * g + 4)), g % 2,
                       xq_fm[:, 512 * g:512 * (g + 1)], "xq_fm", accum=True)

        def emit_mm1_half(i, half):
            for d in range(8):
                for s4 in range(4):
                    sup = 4 * half + s4
                    b.op("tensor", lambda t, d=d, s4=s4, sup=sup: t.matmul(
                        psc[s4][:, :], xq_fm[:, 128 * d:128 * (d + 1)],
                        w1tb(d)[:, 512 * sup:512 * (sup + 1)],
                        start=(d == 0), stop=False),
                        reads=["xq_fm", f"w1q{d}"], writes=[f"psc{s4}"],
                        inc=False)
            for s4 in range(4):
                sup = 4 * half + s4
                b.op("tensor", lambda t, s4=s4, sup=sup, i=i: t.matmul(
                    psc[s4][:, :], s1row2(i % 2),
                    b1row2[:, 512 * sup:512 * (sup + 1)],
                    start=False, stop=True),
                    reads=[f"s1row2{i % 2}", "b1row2"], writes=[f"psc{s4}"])
            for s4 in range(4):
                sup = 4 * half + s4
                b.op("scalar", lambda s, s4=s4, sup=sup, i=i: s.activation(
                    hbuf(512 * sup, 512 * (sup + 1)), psc[s4][:, :],
                    AF.Gelu, scale=s1c[:, i % 2:i % 2 + 1]),
                    reads=[f"psc{s4}", f"s1c{i % 2}"], writes=[f"hb{sup}"])
            for s4 in range(4):
                sup = 4 * half + s4
                b.op("vector", lambda v, sup=sup: v.reduce_max(
                    habs[:, sup:sup + 1], hbuf(512 * sup, 512 * (sup + 1)),
                    axis=AXX, apply_absolute_value=True),
                    reads=[f"hb{sup}"], accum_writes=["habs"])

        def emit_roundh(i):
            b.op("vector", lambda v: v.reduce_max(hsr, habs[:, 0:8], axis=AXX),
                 reads=["habs"], writes=["hsr"])
            b.op("vector", lambda v: v.tensor_scalar(
                hsr, hsr, EPS, None, ALU.max), reads=["hsr"], writes=["hsr"])
            b.op("vector", lambda v: v.reciprocal(ch, hsr),
                 reads=["hsr"], writes=["ch"])
            b.op("vector", lambda v: v.tensor_scalar(
                ch, ch, QB, None, ALU.mult), reads=["ch"], writes=["ch"])
            b.op("vector", lambda v: v.tensor_scalar(
                t2, ch, w2si, None, ALU.mult),
                reads=["ch", "w2si"], writes=["t2"])
            b.op("vector", lambda v: v.tensor_copy(s2hl[:, 0:1], t2),
                 reads=["t2"], writes=["s2hl"])
            b.op("vector", lambda v: v.scalar_tensor_tensor(
                s2hl[:, 1:2], t2, 1.0, s2hl[:, 0:1], ALU.mult, ALU.subtract),
                reads=["t2", "s2hl"], writes=["s2hl"])
            b.op("vector", lambda v, i=i: v.tensor_scalar(
                s2c[:, i % 2:i % 2 + 1], hsr, w2sb, 1.0 / QB, ALU.mult,
                ALU.mult),
                reads=["hsr", "w2sb"], writes=[f"s2c{i % 2}"])
            rd = [f"hb{s}" for s in range(8)]
            b.op("vector", lambda v: v.tensor_scalar(
                hbuf(0, 4096), hbuf(0, 4096), ch, rcd, ALU.mult, ALU.add),
                reads=rd + ["ch", "rcd"], writes=rd)
            b.op("vector", lambda v, i=i: v.tensor_scalar(
                hq_tm(i % 2), hbuf(0, 4096), rcd, None, ALU.subtract),
                reads=rd + ["rcd"], writes=[f"hqtm{i % 2}"])

        def emit_s2rows(i):
            b.op("tensor", lambda t: t.transpose(
                ptp[1][0:2, 0:128], s2hl, ident[:, :]),
                reads=["s2hl", "ident"], writes=["ptp1"])
            b.op("scalar", lambda s, i=i: s.copy(s2row2(i % 2),
                                                 ptp[1][0:2, 0:128]),
                 reads=["ptp1"], writes=[f"s2row2{i % 2}"])

        def emit_hqT_half(i, half):
            sl = i % 2
            for g in range(4 * half, 4 * half + 4):
                tgroup(hq_tm(sl), f"hqtm{sl}", list(range(4 * g, 4 * g + 4)),
                       g % 2, hq_fm[:, 512 * g:512 * (g + 1)], "hq_fm",
                       accum=True)

        def emit_mm2(i):
            for c in range(32):
                b.op("tensor", lambda t, c=c: t.matmul(
                    pav[:, 0:512], hq_fm[:, 128 * c:128 * (c + 1)],
                    w2tb(c)[:, 0:512], start=(c == 0), stop=False),
                    reads=["hq_fm", f"w2q{c}"], writes=["pav0"], inc=False)
                b.op("tensor", lambda t, c=c: t.matmul(
                    pav[:, 512:1024], hq_fm[:, 128 * c:128 * (c + 1)],
                    w2tb(c)[:, 512:1024], start=(c == 0), stop=False),
                    reads=["hq_fm", f"w2q{c}"], writes=["pav1"], inc=False)
            b.op("tensor", lambda t, i=i: t.matmul(
                pav[:, 0:512], s2row2(i % 2), b2row2[:, 0:512],
                start=False, stop=True),
                reads=[f"s2row2{i % 2}", "b2row2"], writes=["pav0"])
            b.op("tensor", lambda t, i=i: t.matmul(
                pav[:, 512:1024], s2row2(i % 2), b2row2[:, 512:1024],
                start=False, stop=True),
                reads=[f"s2row2{i % 2}", "b2row2"], writes=["pav1"])

        def emit_epiB(i):
            sl = i % 3
            b.op("vector", lambda v, i=i, sl=sl: v.tensor_scalar(
                y1t(sl), y1t(sl), rstdA[:, i:i + 1], nmrA[:, i:i + 1],
                ALU.mult, ALU.add),
                reads=[f"y1t{sl}", "rstdA", "nmrA"], writes=[f"y1t{sl}"])
            b.op("vector", lambda v, i=i, sl=sl: v.scalar_tensor_tensor(
                y2st(i % 2), pav[:, :], s2c[:, i % 2:i % 2 + 1], y1t(sl),
                ALU.mult, ALU.add, accum_out=rows),
                reads=["pav0", "pav1", f"s2c{i % 2}", f"y1t{sl}"],
                writes=[f"y2st{i % 2}", "rows"])
            b.op("scalar", lambda s, i=i: s.activation(
                scr1, y2st(i % 2), AF.Square, accum_out=ssq),
                reads=[f"y2st{i % 2}"], writes=["scr1", "ssq"])
            b.op("vector", lambda v, i=i: v.tensor_scalar(
                mB[:, i:i + 1], rows, 1.0 / 1024.0, None, ALU.mult),
                reads=["rows"], accum_writes=["mB"])
            b.op("vector", lambda v, i=i: v.tensor_tensor(
                t2, mB[:, i:i + 1], mB[:, i:i + 1], ALU.mult),
                reads=["mB"], writes=["t2"])
            b.op("vector", lambda v: v.tensor_scalar(
                t3, ssq, 1.0 / 1024.0, None, ALU.mult),
                reads=["ssq"], writes=["t3"])
            b.op("vector", lambda v, i=i: v.scalar_tensor_tensor(
                uB[:, i:i + 1], t3, EPS, t2, ALU.add, ALU.subtract),
                reads=["t3", "t2"], accum_writes=["uB"])
            b.dma("sync", y2pre_h[128 * i:128 * (i + 1), :], y2st(i % 2),
                  reads=[f"y2st{i % 2}"], writes=[f"y2h{i}"])

        b.dma("gpsimd", y1t(0), y1pre_h[0:128, :], reads=["y1h0"],
              writes=["y1t0"])
        if NT > 1:
            b.dma("gpsimd", y1t(1), y1pre_h[128:256, :], reads=["y1h1"],
                  writes=["y1t1"])
        emit_quant(0)
        emit_xqT(0)
        emit_s1rows(0)
        for i in range(NT + 1):
            if i < NT:
                emit_mm1_half(i, 0)
                if i >= 1:
                    emit_hqT_half(i - 1, 0)
                if i + 1 < NT:
                    emit_quant(i + 1)
                emit_mm1_half(i, 1)
                if i >= 1:
                    emit_hqT_half(i - 1, 1)
                emit_roundh(i)
                if i + 1 < NT:
                    emit_xqT(i + 1)
            elif i >= 1:
                emit_hqT_half(i - 1, 0)
                emit_hqT_half(i - 1, 1)
            if i >= 1:
                emit_mm2(i - 1)
                emit_epiB(i - 1)
            if i < NT:
                if i + 1 < NT:
                    emit_s1rows(i + 1)
                emit_s2rows(i)
            if i + 2 < NT:
                b.dma("gpsimd", y1t((i + 2) % 3),
                      y1pre_h[128 * (i + 2):128 * (i + 3), :],
                      reads=[f"y1h{i + 2}"], writes=[f"y1t{(i + 2) % 3}"])

        b.op("scalar", lambda s: s.activation(lnu[:, 0:NT], uB[:, 0:NT], AF.Ln),
             reads=["uB"], writes=["lnu"])
        b.op("scalar", lambda s: s.activation(rstdB[:, 0:NT], lnu[:, 0:NT],
                                              AF.Exp, scale=-0.5),
             reads=["lnu"], writes=["rstdB"])
        b.op("vector", lambda v: v.tensor_tensor(nmrB[:, 0:NT], mB[:, 0:NT],
                                                 rstdB[:, 0:NT], ALU.mult),
             reads=["mB", "rstdB"], writes=["nmrB"])
        b.op("vector", lambda v: v.tensor_scalar(nmrB[:, 0:NT], nmrB[:, 0:NT],
                                                 -1.0, None, ALU.mult),
             reads=["nmrB"], writes=["nmrB"])
        b.dma("scalar", y2st(0), y2pre_h[0:128, :], reads=["y2h0"],
              writes=["y2st0"])
        if NT > 1:
            b.dma("scalar", y2st(1), y2pre_h[128:256, :], reads=["y2h1"],
                  writes=["y2st1"])
        for i in range(NT):
            sl = i % 2
            ob_rg = [f"hb{2 * sl}", f"hb{2 * sl + 1}"]
            b.op("vector", lambda v, i=i, sl=sl: v.tensor_scalar(
                outb(sl), y2st(sl), rstdB[:, i:i + 1], nmrB[:, i:i + 1],
                ALU.mult, ALU.add),
                reads=[f"y2st{sl}", "rstdB", "nmrB"], writes=ob_rg)
            b.dma("sync", out[128 * i:128 * (i + 1), :], outb(sl),
                  reads=ob_rg, writes=[f"oh{i}"])
            if i + 2 < NT:
                b.dma("scalar", y2st(sl), y2pre_h[128 * (i + 2):128 * (i + 3), :],
                      reads=[f"y2h{i + 2}"], writes=[f"y2st{sl}"])
        b.drain_dma("sync")
        b.emit(block)
    return nc


def make_inputs(L0_attn_encoding, L0_attn_R, L0_ffn_W1, L0_ffn_b1, L0_ffn_W2,
                L0_ffn_b2):
    ident = np.eye(128, dtype=ml_dtypes.bfloat16)
    u = np.arange(896, dtype=np.int64)[None, :]
    i = np.arange(128, dtype=np.int64)[:, None]
    maskc = np.where(u <= 384 + i, 0.0, NEG).astype(np.float32)
    ones = np.ones((128, 1), np.float32)
    w1t = np.ascontiguousarray(np.asarray(L0_ffn_W1).T)
    w2t = np.ascontiguousarray(np.asarray(L0_ffn_W2).T)
    b1bc = np.ascontiguousarray(
        np.broadcast_to(np.asarray(L0_ffn_b1)[None, :], (128, 4096)))
    b2bc = np.ascontiguousarray(
        np.broadcast_to(np.asarray(L0_ffn_b2)[None, :], (128, 1024)))
    r = np.ascontiguousarray(np.asarray(L0_attn_R))
    encs = np.asarray(L0_attn_encoding)
    in_maps = []
    for c in range(encs.shape[0]):
        in_maps.append({
            "enc": np.ascontiguousarray(encs[c]),
            "r_mat": r, "w1t": w1t, "w2t": w2t,
            "b1bc": b1bc, "b2bc": b2bc, "maskc": maskc,
            "identi": ident, "ones_i": ones,
        })
    return in_maps


_NC = {}


def kernel(L0_attn_encoding, L0_attn_R, L0_ffn_W1, L0_ffn_b1, L0_ffn_W2,
           L0_ffn_b2):
    in_maps = make_inputs(L0_attn_encoding, L0_attn_R, L0_ffn_W1, L0_ffn_b1,
                          L0_ffn_W2, L0_ffn_b2)
    if "nc" not in _NC:
        _NC["nc"] = build_nc(NT=32)
    res = run_bass_kernel_spmd(_NC["nc"], in_maps, core_ids=list(range(8)))
    return np.stack([np.asarray(r["out"]) for r in res.results], axis=0)


# version 24
# speedup vs baseline: 1.0306x; 1.0306x over previous
import numpy as np
import ml_dtypes
import concourse.bass as bass
import concourse.mybir as mybir
from concourse.bass_utils import run_bass_kernel_spmd

F32 = mybir.dt.float32
BF16 = mybir.dt.bfloat16
AF = mybir.ActivationFunctionType
ALU = mybir.AluOpType
AXX = mybir.AxisListType.X

RC = float(3 * 2 ** 22)
EPS = 1e-5
QB = 127.0
NEG = -1e9

ENGINES = ("sync", "scalar", "vector", "tensor", "gpsimd")


class Builder:

    def __init__(self, nc, stack):
        self.nc = nc
        self.ops = {e: [] for e in ENGINES}
        self.count = {e: 0 for e in ENGINES}
        self.esem = {e: stack.enter_context(nc.semaphore(f"es_{e}")) for e in ENGINES}
        self.pend_r = {e: [] for e in ENGINES}
        self.pend_w = {e: [] for e in ENGINES}
        self.writers = {}
        self.readers = {}
        self.watermark = {e: {} for e in ENGINES}
        self.pools = {}
        for q, n in (("sync", 7), ("gpsimd", 7), ("scalar", 5)):
            self.pools[q] = {"sems": [stack.enter_context(nc.semaphore(f"dp_{q}{i}"))
                                      for i in range(n)], "n": 0}

    def _wait_of(self, h):
        if h[0] == "eng":
            return (self.esem[h[1]], h[2])
        return (h[1], h[2])

    def _collect(self, eng, reads, writes, accum_writes):
        need = []
        for r in reads:
            need += self.writers.get(r, [])
        for w in writes:
            need += self.writers.get(w, [])
            need += self.readers.get(w, [])
        for w in accum_writes:
            need += self.readers.get(w, [])
        waits = {}
        for h in need:
            sem, val = self._wait_of(h)
            k = id(sem)
            if self.watermark[eng].get(k, 0) >= val:
                continue
            if waits.get(k, (None, 0))[1] < val:
                waits[k] = (sem, val)
        out = list(waits.values())
        for sem, val in out:
            self.watermark[eng][id(sem)] = max(
                self.watermark[eng].get(id(sem), 0), val)
        return out

    def _update(self, handle, reads, writes, accum_writes):
        for r in reads:
            self.readers.setdefault(r, []).append(handle)
        for w in writes:
            self.writers[w] = [handle]
            self.readers[w] = []
        for w in accum_writes:
            self.writers.setdefault(w, []).append(handle)

    def op(self, eng, fn, reads=(), writes=(), accum_writes=(), inc=True):
        waits = self._collect(eng, reads, writes, accum_writes)
        if inc:
            self.count[eng] += 1
            h = ("eng", eng, self.count[eng])
            pr = self.pend_r[eng] + list(reads)
            pw = self.pend_w[eng] + list(writes)
            self.pend_r[eng] = []
            self.pend_w[eng] = []
            self._update(h, pr, pw, accum_writes)
            self.ops[eng].append((waits, fn, (self.esem[eng], 1)))
        else:
            assert not accum_writes
            self.pend_r[eng] += list(reads)
            self.pend_w[eng] += list(writes)
            self.ops[eng].append((waits, fn, None))

    def dma(self, q, out_ap, in_ap, reads=(), writes=(), accum_writes=(),
            transpose=False):
        pool = self.pools[q]
        i = pool["n"]
        pool["n"] += 1
        sems = pool["sems"]
        sem = sems[i % len(sems)]
        cyc = i // len(sems)
        waits = []
        if cyc > 0 and self.watermark[q].get(id(sem), 0) < 16 * cyc:
            waits.append((sem, 16 * cyc))
            self.watermark[q][id(sem)] = 16 * cyc
        waits += self._collect(q, reads, writes, accum_writes)
        h = ("sem", sem, 16 * (cyc + 1))
        self._update(h, reads, writes, accum_writes)
        if transpose:
            fn = lambda e, o=out_ap, ii=in_ap: e.dma_start_transpose(o, ii)
        else:
            fn = lambda e, o=out_ap, ii=in_ap: e.dma_start(o, ii)
        self.ops[q].append((waits, fn, (sem, 16)))

    def _pool_targets(self):
        t = []
        for q, pool in self.pools.items():
            for j, s in enumerate(pool["sems"]):
                uses = (pool["n"] - 1 - j) // len(pool["sems"]) + 1 \
                    if pool["n"] > j else 0
                if uses > 0:
                    t.append((s, 16 * uses))
        return t

    def barrier(self):
        targets = [(self.esem[e2], self.count[e2]) for e2 in ENGINES
                   if self.count[e2] > 0]
        targets += self._pool_targets()
        for e in ENGINES:
            waits = []
            for sem, val in targets:
                if self.watermark[e].get(id(sem), 0) < val:
                    waits.append((sem, val))
                    self.watermark[e][id(sem)] = val
            if waits:
                self.ops[e].append((waits, None, None))

    def drain_dma(self, eng="sync"):
        for sem, val in self._pool_targets():
            if self.watermark[eng].get(id(sem), 0) < val:
                self.ops[eng].append(([(sem, val)], None, None))
                self.watermark[eng][id(sem)] = val

    def emit(self, block):
        def replay(name):
            def go(eng):
                for waits, fn, incspec in self.ops[name]:
                    for sem, val in waits:
                        eng.wait_ge(sem, val)
                    if fn is None:
                        continue
                    inst = fn(eng)
                    if incspec is not None:
                        inst.then_inc(incspec[0], incspec[1])
            return go
        block.sync(replay("sync"))
        block.scalar(replay("scalar"))
        block.vector(replay("vector"))
        block.tensor(replay("tensor"))
        block.gpsimd(replay("gpsimd"))


def build_nc(NT=32, npart=None):
    T = NT * 128
    D = 1024
    H = 4096
    NMAC = max(1, NT // 4)
    NPART = npart if npart else (2 if NT >= 8 else 1)
    import contextlib
    stack = contextlib.ExitStack()
    nc = bass.Bass("TRN2", target_bir_lowering=False)

    enc = nc.dram_tensor("enc", [T, D], F32, kind="ExternalInput")
    r_mat = nc.dram_tensor("r_mat", [D, D], F32, kind="ExternalInput")
    w1t = nc.dram_tensor("w1t", [D, H], F32, kind="ExternalInput")
    w2t = nc.dram_tensor("w2t", [H, D], F32, kind="ExternalInput")
    b1bc = nc.dram_tensor("b1bc", [128, H], F32, kind="ExternalInput")
    b2bc = nc.dram_tensor("b2bc", [128, D], F32, kind="ExternalInput")
    maskc = nc.dram_tensor("maskc", [128, 896], F32, kind="ExternalInput")
    identi = nc.dram_tensor("identi", [128, 128], BF16, kind="ExternalInput")
    ones_i = nc.dram_tensor("ones_i", [128, 1], F32, kind="ExternalInput")
    out = nc.dram_tensor("out", [T, D], F32, kind="ExternalOutput")

    enc_bf = nc.dram_tensor("enc_bf", [T, D], BF16, kind="Internal")
    r_bfT = nc.dram_tensor("r_bfT", [8, 128, D], BF16, kind="Internal")
    y1pre_h = nc.dram_tensor("y1pre_h", [T, D], F32, kind="Internal")
    y2pre_h = nc.dram_tensor("y2pre_h", [T, D], BF16, kind="Internal")
    w1q_h = nc.dram_tensor("w1q_h", [D, H], BF16, kind="Internal")
    w2q_h = nc.dram_tensor("w2q_h", [H, D], BF16, kind="Internal")

    BIGW = max(NT * 2048, 65536)
    with stack:
        bf_big = stack.enter_context(nc.sbuf_tensor("bf_big", [128, BIGW], BF16))
        bf_med = stack.enter_context(nc.sbuf_tensor("bf_med", [128, 16384], BF16))
        f32_big = stack.enter_context(nc.sbuf_tensor("f32_big", [128, 8192], F32))
        b1b = stack.enter_context(nc.sbuf_tensor("b1b", [128, H], BF16))
        rows2 = stack.enter_context(nc.sbuf_tensor("rows2", [2, 1536], BF16))
        hl = stack.enter_context(nc.sbuf_tensor("hl", [128, 4], BF16))
        mask_sb = stack.enter_context(nc.sbuf_tensor("mask_sb", [128, 896], BF16))
        ident = stack.enter_context(nc.sbuf_tensor("ident", [128, 128], BF16))
        ones_sb = stack.enter_context(nc.sbuf_tensor("ones_sb", [128, 1], F32))
        stats = stack.enter_context(nc.sbuf_tensor("stats", [128, 448], F32))
        psc = [stack.enter_context(nc.psum_tensor(f"psc{i}", [128, 512], F32))
               for i in range(4)]
        pav = stack.enter_context(nc.psum_tensor("pav", [128, 1024], F32))
        ptp = [stack.enter_context(nc.psum_tensor(f"ptp{i}", [128, 1024], BF16))
               for i in range(2)]
        block = stack.enter_context(nc.Block())
        b = Builder(nc, stack)

        def etm(k):
            return bf_big[:, 1024 * k:1024 * (k + 1)]

        def efm(c, lo, hi):
            base = NT * 1024 + NT * 128 * c
            return bf_big[:, base + lo:base + hi]
        def w1tb(d):
            return bf_big[:, 4096 * d:4096 * (d + 1)]

        def w2tb(h):
            return bf_big[:, 32768 + 1024 * h:32768 + 1024 * (h + 1)]
        def encR(e):
            return bf_med[:, 512 * e:512 * (e + 1)]

        def A_sb(s):
            return bf_med[:, 4096 + NT * 128 * s:4096 + NT * 128 * (s + 1)]
        abase = 4096 + 2 * NT * 128

        def ATg(s):
            if s < 2:
                return bf_med[:, abase + 512 * s:abase + 512 * (s + 1)]
            return b1b[:, 512 * (s - 2):512 * (s - 1)]
        rstage = bf_med[:, abase + 1024:abase + 2048]
        rstage2 = b1b[:, 3072:4096]

        def Rcol(s):
            return bf_med[:, abase + 2048 + 1024 * s:abase + 2048 + 1024 * (s + 1)]
        hq_fm = bf_med[:, 0:4096]
        xq_tm = bf_med[:, 4096:5120]
        xq_fm = bf_med[:, 5120:6144]

        def y2st(s):
            return bf_med[:, 6144 + 1024 * s:6144 + 1024 * (s + 1)]

        def hq_tm(s):
            return bf_med[:, 8192 + 4096 * s:8192 + 4096 * (s + 1)]
        def S_sl(lo, hi):
            return f32_big[:, lo:hi]

        def eres(s):
            return f32_big[:, 4096 + 1024 * s:4096 + 1024 * (s + 1)]

        def y1p(s):
            return f32_big[:, 6144 + 1024 * s:6144 + 1024 * (s + 1)]
        def wfp(s):
            return f32_big[:, 4096 * s:4096 * (s + 1)]
        def hbuf(lo, hi):
            return f32_big[:, lo:hi]

        def y1t(s):
            off = [4096, 5120, 7168][s]
            return f32_big[:, off:off + 1024]
        scr1 = f32_big[:, 6144:7168]

        def outb(s):
            return f32_big[:, 1024 * s:1024 * (s + 1)]

        SC = {}
        _sc = [0]

        def st(name, w=1):
            assert name not in SC
            SC[name] = _sc[0]
            _sc[0] += w
            o = SC[name]
            return stats[:, o:o + w]
        smax = st("smax", 8)
        sums4 = st("sums4", 8)
        bcn = st("bcn")
        sums = st("sums", 2)
        rcp = st("rcp", 2)
        rows = st("rows")
        ssq = st("ssq")
        t2 = st("t2")
        t3 = st("t3")
        mA = st("mA", NT)
        uA = st("uA", NT)
        mB = st("mB", NT)
        uB = st("uB", NT)
        rstdA = st("rstdA", NT)
        nmrA = st("nmrA", NT)
        rstdB = st("rstdB", NT)
        nmrB = st("nmrB", NT)
        lnu = st("lnu", NT)
        colab = st("colab", 32)
        colab2 = st("colab2", 32)
        w1sb = st("w1sb")
        w1si = st("w1si")
        w1hi = st("w1hi")
        w1lo = st("w1lo")
        w2sb = st("w2sb")
        w2si = st("w2si")
        w2hi = st("w2hi")
        w2lo = st("w2lo")
        habs = st("habs", 8)
        xsr = st("xsr")
        cx = st("cx")
        s1c = st("s1c", 2)
        hsr = st("hsr")
        ch = st("ch")
        s2c = st("s2c", 2)
        csum = st("csum")
        rcd = st("rcd")
        assert _sc[0] <= 448

        def part_of(tok):
            return 0 if NPART == 1 else (0 if tok < T // 2 else 1)

        b.op("vector", lambda v: v.memset(bcn, -12.0), writes=["bcn"])
        b.op("vector", lambda v: v.memset(rcd, 1.0), writes=["rcd"])
        b.op("vector", lambda v: v.tensor_scalar(
            rcd, rcd, 2.0 ** -24, 1.0, ALU.add, ALU.subtract),
            reads=["rcd"], writes=["rcd"])
        b.op("vector", lambda v: v.tensor_scalar(
            rcd, rcd, 2.0 ** 24, None, ALU.mult),
            reads=["rcd"], writes=["rcd"])
        b.op("vector", lambda v: v.tensor_scalar(
            rcd, rcd, 6755399441055744.0, RC, ALU.mult, ALU.add),
            reads=["rcd"], writes=["rcd"])


        NG = NT // 4

        def emit_enc_loads(g):
            if not (0 <= g < NG):
                return
            for k in range(4 * g, 4 * g + 4):
                b.dma("gpsimd", etm(k), enc[128 * k:128 * (k + 1), :],
                      writes=[f"etm{k}"])

        def emit_enc_stores(g):
            if not (0 <= g < NG):
                return
            for k in range(4 * g, 4 * g + 4):
                b.dma("sync", enc_bf[128 * k:128 * (k + 1), :], etm(k),
                      reads=[f"etm{k}"], accum_writes=[f"encbf{g}"])

        def emit_enc_transp(g):
            if not (0 <= g < NG):
                return
            for c in range(8):
                b.dma("sync", efm(c, 512 * g, 512 * (g + 1)),
                      enc_bf[512 * g:512 * (g + 1), 128 * c:128 * (c + 1)],
                      reads=[f"encbf{g}"], writes=[f"efm{c}g{g}"],
                      transpose=True)

        b.dma("gpsimd", mask_sb[:, :], maskc[:, :], writes=["mask"])
        emit_enc_loads(0)
        b.dma("sync", ident[:, :], identi[:, :], writes=["ident"])
        b.dma("sync", ones_sb[:, :], ones_i[:, :], writes=["ones"])
        for d in range(8):
            stg, rg = (rstage, "rstage") if d % 2 == 0 else (rstage2, "rstage2")
            b.dma("gpsimd", stg[:, :], r_mat[128 * d:128 * (d + 1), :],
                  writes=[rg])
            b.dma("scalar",
                  r_bfT[:, :, 128 * d:128 * (d + 1)].transpose([1, 0, 2]),
                  stg.rearrange("p (e j) -> p e j", e=8),
                  reads=[rg], accum_writes=["rbfT"])
        emit_enc_loads(1)
        emit_enc_stores(0)
        emit_enc_transp(0)
        b.dma("sync", Rcol(0), r_bfT[0], reads=["rbfT"], writes=["Rcol0"])
        emit_enc_loads(2)
        emit_enc_stores(1)
        emit_enc_transp(1)
        emit_enc_loads(3)
        emit_enc_stores(2)
        b.dma("gpsimd", eres(0), enc[0:128, :], writes=["eres0"])
        rload = [1]

        def wslot(s):
            return f32_big[:, 1024 * s:1024 * (s + 1)]

        def w1_chunk_aps(ci):
            r, q = divmod(ci, 4)
            src_ = w1t[128 * r:128 * (r + 1), 1024 * q:1024 * (q + 1)]
            dst = w1q_h[128 * r:128 * (r + 1), 1024 * q:1024 * (q + 1)]
            return src_, dst

        def w2_chunk_aps(ci):
            return (w2t[128 * ci:128 * (ci + 1), :],
                    w2q_h[128 * ci:128 * (ci + 1), :])

        b2row2 = rows2[:, 0:1024]
        def s1row2(s):
            return rows2[:, 1024 + 128 * s:1024 + 128 * (s + 1)]
        def s2row2(s):
            return rows2[:, 1280 + 128 * s:1280 + 128 * (s + 1)]
        b1row2 = b1b[0:2, :]
        s1hl = hl[:, 0:2]
        s2hl = hl[:, 2:4]

        prep_loads = []
        prep_comps = []
        prep_stores = []

        def add_p1(caps, ci, col_ap, colrg, k):
            sl = k % 3
            prep_loads.append(lambda: b.dma(
                "gpsimd", wslot(sl), caps(ci)[0], writes=[f"wsl{sl}"]))
            prep_comps.append(lambda: b.op("vector", lambda v: v.reduce_sum(
                col_ap, wslot(sl), axis=AXX, apply_absolute_value=True),
                reads=[f"wsl{sl}"], accum_writes=[colrg]))

        def add_scales(qn, col_all, colrg, nelem, sb_sc, sb_inv, sb_hi, sb_lo):
            def step():
                b.op("vector", lambda v: v.reduce_sum(
                    csum, col_all, axis=AXX), reads=[colrg], writes=["csum"])
                b.op("tensor", lambda t: t.matmul(
                    psc[3][0:1, 0:1], ones_sb[:, :], csum,
                    start=True, stop=True),
                    reads=["ones", "csum"], writes=["psc3"])
                b.op("vector", lambda v: v.tensor_scalar(
                    csum[0:1, 0:1], psc[3][0:1, 0:1], 1.0 / nelem, EPS,
                    ALU.mult, ALU.max), reads=["psc3"], writes=["csum"])
                b.op("tensor", lambda t: t.matmul(
                    psc[3][:, 0:1], ones_sb[0:1, 0:1].broadcast_to((1, 128)),
                    csum[0:1, 0:1], start=True, stop=True),
                    reads=["ones", "csum"], writes=["psc3"])
                b.op("vector", lambda v: v.tensor_copy(sb_sc, psc[3][:, 0:1]),
                     reads=["psc3"], writes=[f"{qn}sb"])
                b.op("vector", lambda v: v.reciprocal(sb_inv, sb_sc),
                     reads=[f"{qn}sb"], writes=[f"{qn}si"])
                b.op("vector", lambda v: v.tensor_scalar(
                    sb_hi, sb_sc, 1.0, None, ALU.mult),
                    reads=[f"{qn}sb"], writes=[f"{qn}hi"])
                b.op("vector", lambda v: v.tensor_scalar(
                    sb_lo, sb_sc, -1.0, None, ALU.mult),
                    reads=[f"{qn}sb"], writes=[f"{qn}lo"])
            prep_loads.append(None)
            prep_comps.append(step)

        def add_p2(qn, caps, ci, sb_inv, sb_hi, sb_lo, k):
            sl = k % 3
            prep_loads.append(lambda: b.dma(
                "gpsimd", wslot(sl), caps(ci)[0], writes=[f"wsl{sl}"]))

            def step():
                b.op("vector", lambda v: v.tensor_scalar(
                    wslot(sl), wslot(sl), sb_hi, sb_lo, ALU.min, ALU.max),
                    reads=[f"wsl{sl}", f"{qn}hi", f"{qn}lo"],
                    writes=[f"wsl{sl}"])
                b.op("vector", lambda v: v.tensor_scalar(
                    wslot(sl), wslot(sl), sb_inv, rcd, ALU.mult, ALU.add),
                    reads=[f"wsl{sl}", f"{qn}si", "rcd"], writes=[f"wsl{sl}"])
                b.op("vector", lambda v: v.tensor_scalar(
                    wslot(sl), wslot(sl), rcd, None, ALU.subtract),
                    reads=[f"wsl{sl}", "rcd"], writes=[f"wsl{sl}"])
                prep_stores.append(lambda: b.dma(
                    "gpsimd", caps(ci)[1], wslot(sl), reads=[f"wsl{sl}"],
                    accum_writes=[f"{qn}qh"]))
            prep_comps.append(step)

        k = 0
        for ci in range(32):
            add_p1(w1_chunk_aps, ci, colab[:, ci:ci + 1], "colab", k)
            k += 1
        add_scales("w1", colab[:, 0:32], "colab", D * H, w1sb, w1si, w1hi, w1lo)
        k += 1
        for ci in range(32):
            add_p1(w2_chunk_aps, ci, colab2[:, ci:ci + 1], "colab2", k)
            k += 1
        add_scales("w2", colab2[:, 0:32], "colab2", D * H, w2sb, w2si, w2hi,
                   w2lo)
        k += 1
        for ci in range(32):
            add_p2("w1", w1_chunk_aps, ci, w1si, w1hi, w1lo, k)
            k += 1
        for ci in range(32):
            add_p2("w2", w2_chunk_aps, ci, w2si, w2hi, w2lo, k)
            k += 1
        NP = len(prep_comps)
        prep_c = [0]
        prep_l = [0]
        store_q = []

        def _flush_stores(upto):
            while store_q and store_q[0][0] <= upto:
                store_q.pop(0)[1]()

        def _issue_load(j):
            _flush_stores(j - 3)
            if prep_loads[j] is not None:
                prep_loads[j]()

        def drain_prep(frac):
            ct = min(int(round(NP * frac)), NP)
            while prep_c[0] < ct:
                kk = prep_c[0]
                while prep_l[0] <= kk:
                    _issue_load(prep_l[0])
                    prep_l[0] += 1
                nst = len(prep_stores)
                prep_comps[kk]()
                for fn in prep_stores[nst:]:
                    store_q.append((kk, fn))
                del prep_stores[nst:]
                prep_c[0] += 1
            while prep_l[0] < min(ct + 2, NP):
                _issue_load(prep_l[0])
                prep_l[0] += 1
            if ct >= NP:
                _flush_stores(NP)

        def emit_encR(m):
            qlo = 512 * m
            for e in range(8):
                if e + 1 < 8 or m + 1 < NMAC:
                    ne = (e + 1) % 8
                    sl = rload[0] % 2
                    b.dma("sync", Rcol(sl), r_bfT[ne], reads=["rbfT"],
                          writes=[f"Rcol{sl}"])
                    rload[0] += 1
                cur = (m * 8 + e) % 2
                bk = 2 + e % 2
                for d in range(8):
                    b.op("tensor",
                         lambda t, e=e, d=d, cur=cur, qlo=qlo, bk=bk: t.matmul(
                             psc[bk][:, :], Rcol(cur)[:, 128 * d:128 * (d + 1)],
                             efm(d, qlo, qlo + 512),
                             start=(d == 0), stop=(d == 7)),
                         reads=[f"Rcol{cur}", f"efm{d}g{m}"],
                         writes=[f"psc{bk}"], inc=(d == 7))
                b.op("scalar", lambda s, e=e, bk=bk: s.copy(encR(e), psc[bk][:, :]),
                     reads=[f"psc{bk}"], writes=[f"encR{e}"])

        def emit_scores_super(qi, sup):
            nsup = (qi + 4) // 4
            off = 128 * qi - 512 * (nsup - 1)
            bank = sup % 4
            p = qi % 2
            for e in range(8):
                b.op("tensor",
                     lambda t, e=e, sup=sup, bank=bank, qi=qi: t.matmul(
                         psc[bank][:, :],
                         encR(e)[:, 128 * (qi % 4):128 * (qi % 4 + 1)],
                         efm(e, 512 * sup, 512 * (sup + 1)),
                         start=(e == 0), stop=(e == 7)),
                     reads=[f"encR{e}", f"efm{e}g{sup}"],
                     writes=[f"psc{bank}"], inc=(e == 7))
            if sup == nsup - 1:
                b.op("vector", lambda v, bank=bank, off=off: v.tensor_tensor(
                    psc[bank][:, :], psc[bank][:, :],
                    mask_sb[:, 384 - off:896 - off], ALU.add),
                    reads=[f"psc{bank}", "mask"], writes=[f"psc{bank}"])
            b.op("scalar", lambda s, sup=sup, bank=bank, p=p:
                 s.activation(A_sb(p)[:, 512 * sup:512 * (sup + 1)],
                              psc[bank][:, :], AF.Exp, bias=bcn,
                              accum_out=sums4[:, sup:sup + 1]),
                 reads=[f"psc{bank}", "bcn"], accum_writes=[f"A{p}", "sums4"])

        def emit_softmax_tail(qi):
            nsup = (qi + 4) // 4
            sl = qi % 2
            b.op("vector", lambda v, nsup=nsup, sl=sl: v.reduce_sum(
                sums[:, sl:sl + 1], sums4[:, 0:nsup], axis=AXX),
                reads=["sums4"], writes=[f"sums{sl}"])
            b.op("vector", lambda v, sl=sl: v.reciprocal(
                rcp[:, sl:sl + 1], sums[:, sl:sl + 1]),
                reads=[f"sums{sl}"], writes=[f"rcp{sl}"])

        def tgroup(src_ap, src_region, blocks, pk, dst_ap, dst_region, accum=False):
            for j, bb in enumerate(blocks):
                b.op("tensor", lambda t, j=j, bb=bb: t.transpose(
                    ptp[pk][:, 128 * j:128 * (j + 1)],
                    src_ap[:, 128 * bb:128 * (bb + 1)], ident[:, :]),
                    reads=[src_region, "ident"], accum_writes=[f"ptp{pk}"])
            n = 128 * len(blocks)
            dst2 = dst_ap[:, 0:n] if dst_ap.shape[1] > n else dst_ap
            if accum:
                b.op("scalar", lambda s, n=n: s.copy(dst2, ptp[pk][:, 0:n]),
                     reads=[f"ptp{pk}"], accum_writes=[dst_region])
            else:
                b.op("scalar", lambda s, n=n: s.copy(dst2, ptp[pk][:, 0:n]),
                     reads=[f"ptp{pk}"], writes=[dst_region])

        def emit_avM(qi, groups, glo, ghi):
            nb = qi + 1
            for g in range(glo, min(ghi, len(groups))):
                for j, bb in enumerate(groups[g]):
                    b.op("tensor", lambda t, j=j, bb=bb, g=g, nb=nb: t.matmul(
                        pav[:, 0:512], ATg(g)[:, 128 * j:128 * (j + 1)],
                        etm(bb)[:, 0:512],
                        start=(bb == 0), stop=(bb == nb - 1)),
                        reads=[f"ATG{g}", f"etm{bb}"], writes=["pav0"],
                        inc=(bb == nb - 1))
                    b.op("tensor", lambda t, j=j, bb=bb, g=g, nb=nb: t.matmul(
                        pav[:, 512:1024], ATg(g)[:, 128 * j:128 * (j + 1)],
                        etm(bb)[:, 512:1024],
                        start=(bb == 0), stop=(bb == nb - 1)),
                        reads=[f"ATG{g}", f"etm{bb}"], writes=["pav1"],
                        inc=(bb == nb - 1))

        def emit_epiA(qi):
            sl = qi % 2
            b.op("vector", lambda v, sl=sl: v.scalar_tensor_tensor(
                y1p(sl), pav[:, :], rcp[:, sl:sl + 1], eres(sl),
                ALU.mult, ALU.add, accum_out=rows),
                reads=["pav0", "pav1", f"rcp{sl}", f"eres{sl}"],
                writes=[f"y1p{sl}", "rows"])
            b.op("scalar", lambda s, sl=sl: s.activation(
                rstage, y1p(sl), AF.Square, accum_out=ssq),
                reads=[f"y1p{sl}"], writes=["rstage", "ssq"])
            b.op("vector", lambda v, qi=qi: v.tensor_scalar(
                mA[:, qi:qi + 1], rows, 1.0 / 1024.0, None, ALU.mult),
                reads=["rows"], accum_writes=["mA"])
            b.op("vector", lambda v, qi=qi: v.tensor_tensor(
                t2, mA[:, qi:qi + 1], mA[:, qi:qi + 1], ALU.mult),
                reads=["mA"], writes=["t2"])
            b.op("vector", lambda v: v.tensor_scalar(
                t3, ssq, 1.0 / 1024.0, None, ALU.mult),
                reads=["ssq"], writes=["t3"])
            b.op("vector", lambda v, qi=qi: v.scalar_tensor_tensor(
                uA[:, qi:qi + 1], t3, EPS, t2, ALU.add, ALU.subtract),
                reads=["t3", "t2"], accum_writes=["uA"])
            b.dma("sync", y1pre_h[128 * qi:128 * (qi + 1), :], y1p(sl),
                  reads=[f"y1p{sl}"], writes=[f"y1h{qi}"])

        for qi in range(NT + 1):
            nsup = (qi + 4) // 4 if qi < NT else 0
            if qi < NT and qi % 4 == 0:
                m = qi // 4
                emit_encR(m)
                emit_enc_loads(m + 4)
                emit_enc_stores(m + 3)
                emit_enc_transp(m + 2)
            if 1 <= qi < NT:
                b.dma("gpsimd", eres(qi % 2), enc[128 * qi:128 * (qi + 1), :],
                      writes=[f"eres{qi % 2}"])
            ngp, groups = 0, []
            if qi >= 1:
                nb = qi
                ngp = (nb + 3) // 4
                groups = [list(range(4 * g, min(4 * g + 4, nb)))
                          for g in range(ngp)]
                ar = f"A{(qi - 1) % 2}"
                Ap = A_sb((qi - 1) % 2)
            for s in range(max(nsup, ngp + 1)):
                if s < nsup:
                    emit_scores_super(qi, s)
                if s < ngp:
                    tgroup(Ap, ar, groups[s], s % 2, ATg(s), f"ATG{s}")
                if 0 <= s - 1 < ngp:
                    emit_avM(qi - 1, groups, s - 1, s)
            if qi >= 1:
                emit_epiA(qi - 1)
            if qi < NT:
                emit_softmax_tail(qi)
            if qi >= 1:
                drain_prep((2 * qi + 1) / (2.0 * NT))
            drain_prep((qi + 1) / (1.0 * NT))
        drain_prep(1.0)

        b.barrier()
        b.dma("gpsimd", b1row2, b1bc[0:2, :], writes=["b1row2"])
        b.dma("gpsimd", b2row2, b2bc[0:2, :], writes=["b2row2"])
        for d in range(8):
            b.dma("sync", w1tb(d), w1q_h[128 * d:128 * (d + 1), :],
                  reads=["w1qh"], writes=[f"w1q{d}"])
        for k in range(8):
            dst = bf_big[:, 32768 + 4096 * k:32768 + 4096 * (k + 1)].rearrange(
                "p (r c) -> p r c", r=4)
            src = w2q_h[512 * k:512 * (k + 1), :].rearrange(
                "(r p) c -> p r c", r=4, p=128)
            b.dma("sync", dst, src, reads=["w2qh"],
                  writes=[f"w2q{4 * k + j}" for j in range(4)])
        b.op("scalar", lambda s: s.activation(lnu[:, 0:NT], uA[:, 0:NT], AF.Ln),
             reads=["uA"], writes=["lnu"])
        b.op("scalar", lambda s: s.activation(rstdA[:, 0:NT], lnu[:, 0:NT],
                                              AF.Exp, scale=-0.5),
             reads=["lnu"], writes=["rstdA"])
        b.op("vector", lambda v: v.tensor_tensor(nmrA[:, 0:NT], mA[:, 0:NT],
                                                 rstdA[:, 0:NT], ALU.mult),
             reads=["mA", "rstdA"], writes=["nmrA"])
        b.op("vector", lambda v: v.tensor_scalar(nmrA[:, 0:NT], nmrA[:, 0:NT],
                                                 -1.0, None, ALU.mult),
             reads=["nmrA"], writes=["nmrA"])

        def emit_quant(i):
            sl = i % 3
            r = rstdA[:, i:i + 1]
            n = nmrA[:, i:i + 1]
            b.op("vector", lambda v, sl=sl: v.reduce_max(
                xsr, y1t(sl), axis=AXX), reads=[f"y1t{sl}", "rstdA"],
                writes=["xsr"])
            b.op("vector", lambda v, sl=sl: v.tensor_reduce(
                t2, y1t(sl), AXX, ALU.min), reads=[f"y1t{sl}"], writes=["t2"])
            b.op("vector", lambda v, r=r, n=n: v.tensor_scalar(
                xsr, xsr, r, n, ALU.mult, ALU.add),
                reads=["xsr", "nmrA"], writes=["xsr"])
            b.op("vector", lambda v, r=r, n=n: v.tensor_scalar(
                t2, t2, r, n, ALU.mult, ALU.add), reads=["t2"], writes=["t2"])
            b.op("vector", lambda v: v.tensor_scalar(
                t2, t2, -1.0, None, ALU.mult), reads=["t2"], writes=["t2"])
            b.op("vector", lambda v: v.tensor_tensor(
                xsr, xsr, t2, ALU.max), reads=["xsr", "t2"], writes=["xsr"])
            b.op("vector", lambda v: v.tensor_scalar(
                xsr, xsr, EPS, None, ALU.max), reads=["xsr"], writes=["xsr"])
            b.op("vector", lambda v: v.reciprocal(cx, xsr),
                 reads=["xsr"], writes=["cx"])
            b.op("vector", lambda v: v.tensor_scalar(
                cx, cx, QB, None, ALU.mult), reads=["cx"], writes=["cx"])
            b.op("vector", lambda v: v.tensor_scalar(
                t2, cx, w1si, None, ALU.mult),
                reads=["cx", "w1si"], writes=["t2"])
            b.op("vector", lambda v: v.tensor_copy(s1hl[:, 0:1], t2),
                 reads=["t2"], writes=["s1hl"])
            b.op("vector", lambda v: v.scalar_tensor_tensor(
                s1hl[:, 1:2], t2, 1.0, s1hl[:, 0:1], ALU.mult, ALU.subtract),
                reads=["t2", "s1hl"], writes=["s1hl"])
            b.op("vector", lambda v, i=i: v.tensor_scalar(
                s1c[:, i % 2:i % 2 + 1], xsr, w1sb, 1.0 / QB, ALU.mult,
                ALU.mult),
                reads=["xsr", "w1sb"], writes=[f"s1c{i % 2}"])
            b.op("vector", lambda v, r=r: v.tensor_tensor(
                t2, cx, r, ALU.mult), reads=["cx"], writes=["t2"])
            b.op("vector", lambda v, n=n: v.tensor_tensor(
                t3, cx, n, ALU.mult), reads=["cx", "nmrA"], writes=["t3"])
            b.op("vector", lambda v, sl=sl: v.tensor_scalar(
                scr1, y1t(sl), t2, t3, ALU.mult, ALU.add),
                reads=[f"y1t{sl}", "t2", "t3"], writes=["scr1"])
            b.op("vector", lambda v: v.tensor_scalar(
                xq_tm, scr1, rcd, rcd, ALU.add, ALU.subtract),
                reads=["scr1", "rcd"], writes=["xq_tm"])

        def emit_s1rows(i):
            b.op("tensor", lambda t: t.transpose(
                ptp[0][0:2, 0:128], s1hl, ident[:, :]),
                reads=["s1hl", "ident"], writes=["ptp0"])
            b.op("scalar", lambda s, i=i: s.copy(s1row2(i % 2),
                                                 ptp[0][0:2, 0:128]),
                 reads=["ptp0"], writes=[f"s1row2{i % 2}"])

        def emit_xqT(i):
            for g in range(2):
                tgroup(xq_tm, "xq_tm", list(range(4 * g, 4 * g + 4)), g % 2,
                       xq_fm[:, 512 * g:512 * (g + 1)], "xq_fm", accum=True)

        def emit_mm1_half(i, half):
            for s4 in range(4):
                sup = 4 * half + s4
                for d in range(8):
                    b.op("tensor", lambda t, d=d, s4=s4, sup=sup: t.matmul(
                        psc[s4][:, :], xq_fm[:, 128 * d:128 * (d + 1)],
                        w1tb(d)[:, 512 * sup:512 * (sup + 1)],
                        start=(d == 0), stop=False),
                        reads=["xq_fm", f"w1q{d}"], writes=[f"psc{s4}"],
                        inc=False)
                b.op("tensor", lambda t, s4=s4, sup=sup, i=i: t.matmul(
                    psc[s4][:, :], s1row2(i % 2),
                    b1row2[:, 512 * sup:512 * (sup + 1)],
                    start=False, stop=True),
                    reads=[f"s1row2{i % 2}", "b1row2"], writes=[f"psc{s4}"])
                b.op("scalar", lambda s, s4=s4, sup=sup, i=i: s.activation(
                    hbuf(512 * sup, 512 * (sup + 1)), psc[s4][:, :],
                    AF.Gelu, scale=s1c[:, i % 2:i % 2 + 1]),
                    reads=[f"psc{s4}", f"s1c{i % 2}"], writes=[f"hb{sup}"])
                b.op("vector", lambda v, sup=sup: v.reduce_max(
                    habs[:, sup:sup + 1], hbuf(512 * sup, 512 * (sup + 1)),
                    axis=AXX, apply_absolute_value=True),
                    reads=[f"hb{sup}"], accum_writes=["habs"])

        def emit_roundh(i):
            b.op("vector", lambda v: v.reduce_max(hsr, habs[:, 0:8], axis=AXX),
                 reads=["habs"], writes=["hsr"])
            b.op("vector", lambda v: v.tensor_scalar(
                hsr, hsr, EPS, None, ALU.max), reads=["hsr"], writes=["hsr"])
            b.op("vector", lambda v: v.reciprocal(ch, hsr),
                 reads=["hsr"], writes=["ch"])
            b.op("vector", lambda v: v.tensor_scalar(
                ch, ch, QB, None, ALU.mult), reads=["ch"], writes=["ch"])
            b.op("vector", lambda v: v.tensor_scalar(
                t2, ch, w2si, None, ALU.mult),
                reads=["ch", "w2si"], writes=["t2"])
            b.op("vector", lambda v: v.tensor_copy(s2hl[:, 0:1], t2),
                 reads=["t2"], writes=["s2hl"])
            b.op("vector", lambda v: v.scalar_tensor_tensor(
                s2hl[:, 1:2], t2, 1.0, s2hl[:, 0:1], ALU.mult, ALU.subtract),
                reads=["t2", "s2hl"], writes=["s2hl"])
            b.op("vector", lambda v, i=i: v.tensor_scalar(
                s2c[:, i % 2:i % 2 + 1], hsr, w2sb, 1.0 / QB, ALU.mult,
                ALU.mult),
                reads=["hsr", "w2sb"], writes=[f"s2c{i % 2}"])
            rd = [f"hb{s}" for s in range(8)]
            b.op("vector", lambda v: v.tensor_scalar(
                hbuf(0, 4096), hbuf(0, 4096), ch, rcd, ALU.mult, ALU.add),
                reads=rd + ["ch", "rcd"], writes=rd)
            b.op("vector", lambda v, i=i: v.tensor_scalar(
                hq_tm(i % 2), hbuf(0, 4096), rcd, None, ALU.subtract),
                reads=rd + ["rcd"], writes=[f"hqtm{i % 2}"])

        def emit_s2rows(i):
            b.op("tensor", lambda t: t.transpose(
                ptp[1][0:2, 0:128], s2hl, ident[:, :]),
                reads=["s2hl", "ident"], writes=["ptp1"])
            b.op("scalar", lambda s, i=i: s.copy(s2row2(i % 2),
                                                 ptp[1][0:2, 0:128]),
                 reads=["ptp1"], writes=[f"s2row2{i % 2}"])

        def emit_hqT_half(i, half):
            sl = i % 2
            for g in range(4 * half, 4 * half + 4):
                tgroup(hq_tm(sl), f"hqtm{sl}", list(range(4 * g, 4 * g + 4)),
                       g % 2, hq_fm[:, 512 * g:512 * (g + 1)], "hq_fm",
                       accum=True)

        def emit_mm2(i):
            for c in range(32):
                b.op("tensor", lambda t, c=c: t.matmul(
                    pav[:, 0:512], hq_fm[:, 128 * c:128 * (c + 1)],
                    w2tb(c)[:, 0:512], start=(c == 0), stop=False),
                    reads=["hq_fm", f"w2q{c}"], writes=["pav0"], inc=False)
                b.op("tensor", lambda t, c=c: t.matmul(
                    pav[:, 512:1024], hq_fm[:, 128 * c:128 * (c + 1)],
                    w2tb(c)[:, 512:1024], start=(c == 0), stop=False),
                    reads=["hq_fm", f"w2q{c}"], writes=["pav1"], inc=False)
            b.op("tensor", lambda t, i=i: t.matmul(
                pav[:, 0:512], s2row2(i % 2), b2row2[:, 0:512],
                start=False, stop=True),
                reads=[f"s2row2{i % 2}", "b2row2"], writes=["pav0"])
            b.op("tensor", lambda t, i=i: t.matmul(
                pav[:, 512:1024], s2row2(i % 2), b2row2[:, 512:1024],
                start=False, stop=True),
                reads=[f"s2row2{i % 2}", "b2row2"], writes=["pav1"])

        def emit_epiB(i):
            sl = i % 3
            b.op("vector", lambda v, i=i, sl=sl: v.tensor_scalar(
                y1t(sl), y1t(sl), rstdA[:, i:i + 1], nmrA[:, i:i + 1],
                ALU.mult, ALU.add),
                reads=[f"y1t{sl}", "rstdA", "nmrA"], writes=[f"y1t{sl}"])
            b.op("vector", lambda v, i=i, sl=sl: v.scalar_tensor_tensor(
                y2st(i % 2), pav[:, :], s2c[:, i % 2:i % 2 + 1], y1t(sl),
                ALU.mult, ALU.add, accum_out=rows),
                reads=["pav0", "pav1", f"s2c{i % 2}", f"y1t{sl}"],
                writes=[f"y2st{i % 2}", "rows"])
            b.op("scalar", lambda s, i=i: s.activation(
                scr1, y2st(i % 2), AF.Square, accum_out=ssq),
                reads=[f"y2st{i % 2}"], writes=["scr1", "ssq"])
            b.op("vector", lambda v, i=i: v.tensor_scalar(
                mB[:, i:i + 1], rows, 1.0 / 1024.0, None, ALU.mult),
                reads=["rows"], accum_writes=["mB"])
            b.op("vector", lambda v, i=i: v.tensor_tensor(
                t2, mB[:, i:i + 1], mB[:, i:i + 1], ALU.mult),
                reads=["mB"], writes=["t2"])
            b.op("vector", lambda v: v.tensor_scalar(
                t3, ssq, 1.0 / 1024.0, None, ALU.mult),
                reads=["ssq"], writes=["t3"])
            b.op("vector", lambda v, i=i: v.scalar_tensor_tensor(
                uB[:, i:i + 1], t3, EPS, t2, ALU.add, ALU.subtract),
                reads=["t3", "t2"], accum_writes=["uB"])
            b.dma("sync", y2pre_h[128 * i:128 * (i + 1), :], y2st(i % 2),
                  reads=[f"y2st{i % 2}"], writes=[f"y2h{i}"])

        b.dma("gpsimd", y1t(0), y1pre_h[0:128, :], reads=["y1h0"],
              writes=["y1t0"])
        if NT > 1:
            b.dma("gpsimd", y1t(1), y1pre_h[128:256, :], reads=["y1h1"],
                  writes=["y1t1"])
        emit_quant(0)
        emit_xqT(0)
        emit_s1rows(0)
        for i in range(NT + 1):
            if i < NT:
                emit_mm1_half(i, 0)
                if i >= 1:
                    emit_hqT_half(i - 1, 0)
                if i + 1 < NT:
                    emit_quant(i + 1)
                emit_mm1_half(i, 1)
                if i >= 1:
                    emit_hqT_half(i - 1, 1)
                emit_roundh(i)
                if i + 1 < NT:
                    emit_xqT(i + 1)
            elif i >= 1:
                emit_hqT_half(i - 1, 0)
                emit_hqT_half(i - 1, 1)
            if i >= 1:
                emit_mm2(i - 1)
                emit_epiB(i - 1)
            if i < NT:
                if i + 1 < NT:
                    emit_s1rows(i + 1)
                emit_s2rows(i)
            if i + 2 < NT:
                b.dma("gpsimd", y1t((i + 2) % 3),
                      y1pre_h[128 * (i + 2):128 * (i + 3), :],
                      reads=[f"y1h{i + 2}"], writes=[f"y1t{(i + 2) % 3}"])

        b.op("scalar", lambda s: s.activation(lnu[:, 0:NT], uB[:, 0:NT], AF.Ln),
             reads=["uB"], writes=["lnu"])
        b.op("scalar", lambda s: s.activation(rstdB[:, 0:NT], lnu[:, 0:NT],
                                              AF.Exp, scale=-0.5),
             reads=["lnu"], writes=["rstdB"])
        b.op("vector", lambda v: v.tensor_tensor(nmrB[:, 0:NT], mB[:, 0:NT],
                                                 rstdB[:, 0:NT], ALU.mult),
             reads=["mB", "rstdB"], writes=["nmrB"])
        b.op("vector", lambda v: v.tensor_scalar(nmrB[:, 0:NT], nmrB[:, 0:NT],
                                                 -1.0, None, ALU.mult),
             reads=["nmrB"], writes=["nmrB"])
        b.dma("scalar", y2st(0), y2pre_h[0:128, :], reads=["y2h0"],
              writes=["y2st0"])
        if NT > 1:
            b.dma("scalar", y2st(1), y2pre_h[128:256, :], reads=["y2h1"],
                  writes=["y2st1"])
        for i in range(NT):
            sl = i % 2
            ob_rg = [f"hb{2 * sl}", f"hb{2 * sl + 1}"]
            b.op("vector", lambda v, i=i, sl=sl: v.tensor_scalar(
                outb(sl), y2st(sl), rstdB[:, i:i + 1], nmrB[:, i:i + 1],
                ALU.mult, ALU.add),
                reads=[f"y2st{sl}", "rstdB", "nmrB"], writes=ob_rg)
            b.dma("sync", out[128 * i:128 * (i + 1), :], outb(sl),
                  reads=ob_rg, writes=[f"oh{i}"])
            if i + 2 < NT:
                b.dma("scalar", y2st(sl), y2pre_h[128 * (i + 2):128 * (i + 3), :],
                      reads=[f"y2h{i + 2}"], writes=[f"y2st{sl}"])
        b.drain_dma("sync")
        b.emit(block)
    return nc


def make_inputs(L0_attn_encoding, L0_attn_R, L0_ffn_W1, L0_ffn_b1, L0_ffn_W2,
                L0_ffn_b2):
    ident = np.eye(128, dtype=ml_dtypes.bfloat16)
    u = np.arange(896, dtype=np.int64)[None, :]
    i = np.arange(128, dtype=np.int64)[:, None]
    maskc = np.where(u <= 384 + i, 0.0, NEG).astype(np.float32)
    ones = np.ones((128, 1), np.float32)
    w1t = np.ascontiguousarray(np.asarray(L0_ffn_W1).T)
    w2t = np.ascontiguousarray(np.asarray(L0_ffn_W2).T)
    b1bc = np.ascontiguousarray(
        np.broadcast_to(np.asarray(L0_ffn_b1)[None, :], (128, 4096)))
    b2bc = np.ascontiguousarray(
        np.broadcast_to(np.asarray(L0_ffn_b2)[None, :], (128, 1024)))
    r = np.ascontiguousarray(np.asarray(L0_attn_R))
    encs = np.asarray(L0_attn_encoding)
    in_maps = []
    for c in range(encs.shape[0]):
        in_maps.append({
            "enc": np.ascontiguousarray(encs[c]),
            "r_mat": r, "w1t": w1t, "w2t": w2t,
            "b1bc": b1bc, "b2bc": b2bc, "maskc": maskc,
            "identi": ident, "ones_i": ones,
        })
    return in_maps


_NC = {}


def kernel(L0_attn_encoding, L0_attn_R, L0_ffn_W1, L0_ffn_b1, L0_ffn_W2,
           L0_ffn_b2):
    in_maps = make_inputs(L0_attn_encoding, L0_attn_R, L0_ffn_W1, L0_ffn_b1,
                          L0_ffn_W2, L0_ffn_b2)
    if "nc" not in _NC:
        _NC["nc"] = build_nc(NT=32)
    res = run_bass_kernel_spmd(_NC["nc"], in_maps, core_ids=list(range(8)))
    return np.stack([np.asarray(r["out"]) for r in res.results], axis=0)
